# version 15
# baseline (speedup 1.0000x reference)
import numpy as np
import ml_dtypes
from contextlib import ExitStack

import concourse.bass as bass
import concourse.mybir as mybir
from concourse.bass_utils import run_bass_kernel_spmd

F32 = mybir.dt.float32
BF16 = mybir.dt.bfloat16
AF = mybir.ActivationFunctionType
ALU = mybir.AluOpType
AX = mybir.AxisListType
bfdt = ml_dtypes.bfloat16

NCORES = 8
B, T, D = 16, 2048, 1024
SEQ = B // NCORES
NH, RK = 16, 64
NPAIR = NH // 2
NE = 10
HF = 512
TQ = 512
NTQ = T // TQ
NTT = T // 128
NDC = D // 128
EPS = 1e-5
SCALE = 1.0 / 32.0

ENGSEM = {"tensor": "s_pe", "vector": "s_dve", "scalar": "s_act",
          "gpsimd": "s_pool", "sync": "s_sp"}


class Buf:
    __slots__ = ("name", "last_w", "readers", "open_write", "dmasem")

    def __init__(self, name):
        self.name = name
        self.last_w = None
        self.readers = []
        self.open_write = False
        self.dmasem = None


class Prog:

    def __init__(self):
        self.counts = {s: 0 for s in ENGSEM.values()}
        self.ops = {e: [] for e in ENGSEM}
        self.guaranteed = {e: {} for e in ENGSEM}
        self.pending_reads = {e: [] for e in ENGSEM}

    def emit(self, eng, fn, reads=(), writes=(), group="both", dma=False):
        waits = {}

        def need(tok):
            if tok is None:
                return
            s, v = tok
            if waits.get(s, -1) < v:
                waits[s] = v

        for b in reads:
            need(b.last_w)
        if group in ("both", "begin"):
            for b in writes:
                need(b.last_w)
                for tok in b.readers:
                    need(tok)
                assert not b.open_write, f"overlapping write groups on {b.name}"
                if group == "begin":
                    b.open_write = True
        g = self.guaranteed[eng]
        wl = [(s, v) for s, v in waits.items() if g.get(s, -1) < v]
        for s, v in wl:
            g[s] = v

        if dma:
            assert len(writes) == 1
            if writes[0].dmasem is None:
                writes[0].dmasem = "dq_" + writes[0].name
                self.counts.setdefault(writes[0].dmasem, 0)
            sem = writes[0].dmasem
            self.counts[sem] += 16
            tok = (sem, self.counts[sem])
            incs = [(sem, 16)]
        elif group in ("both", "end"):
            sem = ENGSEM[eng]
            self.counts[sem] += 1
            tok = (sem, self.counts[sem])
            incs = [(sem, 1)]
        else:
            tok = None
            incs = []

        if tok is None:
            for b in reads:
                self.pending_reads[eng].append(b)
        else:
            for b in self.pending_reads[eng]:
                b.readers.append(tok)
            self.pending_reads[eng] = []
            for b in reads:
                b.readers.append(tok)
        if group in ("both", "end"):
            for b in writes:
                b.last_w = tok
                b.readers = []
                b.open_write = False

        self.ops[eng].append((wl, fn, incs))
        return tok

    def barrier(self):
        snapshot = sorted(self.counts.items())
        for eng in ENGSEM:
            g = self.guaranteed[eng]
            wl = [(s, v) for s, v in snapshot if v > 0 and g.get(s, -1) < v]
            for s, v in wl:
                g[s] = v
            self.ops[eng].append((wl, None, []))


def build_program():
    nc = bass.Bass("TRN2", target_bir_lowering=False, debug=False)
    p = Prog()

    x_h = nc.dram_tensor("x", [SEQ, T, D], F32, kind="ExternalInput").ap()
    qt_h = nc.dram_tensor("qt", [NPAIR, D, 128], F32, kind="ExternalInput").ap()
    kt_h = nc.dram_tensor("kt", [NPAIR, D, 128], F32, kind="ExternalInput").ap()
    vo_h = nc.dram_tensor("vo", [NH, D, D], F32, kind="ExternalInput").ap()
    rw_h = nc.dram_tensor("rw", [D, 8], F32, kind="ExternalInput").ap()
    rb_h = nc.dram_tensor("rb", [128, 8], F32, kind="ExternalInput").ap()
    w1_h = nc.dram_tensor("w1", [NE, D, HF], F32, kind="ExternalInput").ap()
    w3_h = nc.dram_tensor("w3", [NE, D, HF], F32, kind="ExternalInput").ap()
    w2_h = nc.dram_tensor("w2", [NE, HF, D], F32, kind="ExternalInput").ap()
    eye_h = nc.dram_tensor("eye", [128, 128], F32, kind="ExternalInput").ap()
    ones_h = nc.dram_tensor("ones", [128, 4], BF16, kind="ExternalInput").ap()
    orow_h = nc.dram_tensor("orow", [1, 256], BF16, kind="ExternalInput").ap()
    oblk_h = nc.dram_tensor("oblk", [128, 128], BF16, kind="ExternalInput").ap()
    fc_h = nc.dram_tensor("fconst", [128, 3], F32, kind="ExternalInput").ap()
    out_h = nc.dram_tensor("out", [SEQ, T, D], F32, kind="ExternalOutput").ap()
    x1_h = nc.dram_tensor("x1buf", [SEQ, T, D], F32, kind="Internal").ap()

    x1_b = Buf("x1_hbm")
    out_b = Buf("out_hbm")

    with ExitStack() as top:
        def mk(scope, name, shape, dt):
            h = scope.enter_context(nc.sbuf_tensor(name, shape, dt))
            return h, Buf(name)

        psum = []
        for i in range(8):
            h = top.enter_context(nc.psum_tensor(f"ps{i}", [128, 512], F32))
            psum.append((h, Buf(f"ps{i}")))
        pA, pB, pC, pD, pE, pF, pG, pH = psum

        eye, eye_b = mk(top, "eye_sb", [128, 128], F32)
        ones, ones_b = mk(top, "ones_sb", [128, 4], BF16)
        orow, orow_b = mk(top, "orow_sb", [1, 256], BF16)
        rbb, rbb_b = mk(top, "rb_sb", [128, 8], F32)
        rwsb, rwsb_b = mk(top, "rw_sb", [128, NDC * 8], BF16)
        gates, gates_b = mk(top, "gates", [128, 2 * NTT * 8], F32)
        stat, stat_b = mk(top, "stat", [128, 24], F32)
        mv, mv_b = mk(top, "mv", [128, 4], F32)
        rstd, rstd_b = mk(top, "rstd", [128, 4], F32)
        negc, negc_b = mk(top, "negc", [128, NH], F32)
        cg2, cg2_b = mk(top, "cg2", [1, 4 * NH], F32)
        crow, crow_b = mk(top, "crow", [1, 4 * NH], BF16)
        rrow, rrow_b = mk(top, "rrow", [1, 512], BF16)
        f_sb, f_b = mk(top, "f_sb", [128, 512], F32)
        fcst, fcst_b = mk(top, "fconst_sb", [128, 3], F32)
        oblk, oblk_b = mk(top, "oblk_sb", [128, 128], BF16)

        p.emit("sync", lambda e: e.dma_start(out=eye[:], in_=eye_h),
               writes=[eye_b], dma=True)
        p.emit("sync", lambda e: e.dma_start(out=ones[:], in_=ones_h),
               writes=[ones_b], dma=True)
        p.emit("sync", lambda e: e.dma_start(out=orow[:], in_=orow_h),
               writes=[orow_b], dma=True)
        p.emit("sync", lambda e: e.dma_start(out=rbb[:], in_=rb_h),
               writes=[rbb_b], dma=True)
        p.emit("gpsimd", lambda e: e.dma_start(
            out=rwsb[:].rearrange("p (c e) -> p c e", c=NDC),
            in_=rw_h.rearrange("(c p) e -> p c e", p=128)),
            writes=[rwsb_b], dma=True)
        p.emit("sync", lambda e: e.dma_start(out=fcst[:], in_=fc_h),
               writes=[fcst_b], dma=True)
        p.emit("sync", lambda e: e.dma_start(out=oblk[:], in_=oblk_h),
               writes=[oblk_b], dma=True)

        def emit_ln_tile(xi, xi_b, outs, par):
            so = par * 12
            p.emit("vector", lambda e, xi=xi, so=so: e.bn_stats(
                stat[:, so:so + 6], xi[:, 0:512]),
                reads=[xi_b], writes=[stat_b])
            p.emit("vector", lambda e, xi=xi, so=so: e.bn_stats(
                stat[:, so + 6:so + 12], xi[:, 512:1024]),
                reads=[xi_b], writes=[stat_b])
            p.emit("vector", lambda e, par=par, so=so: e.bn_aggr(
                mv[:, 2 * par:2 * par + 2], stat[:, so:so + 12]),
                reads=[stat_b], writes=[mv_b])
            p.emit("scalar", lambda e, par=par: e.activation(
                out=rstd[:, 2 * par:2 * par + 1], in_=mv[:, 2 * par + 1:2 * par + 2],
                func=AF.Ln, bias=fcst[:, 0:1]),
                reads=[mv_b, fcst_b], writes=[rstd_b])
            p.emit("scalar", lambda e, par=par: e.activation(
                out=rstd[:, 2 * par:2 * par + 1], in_=rstd[:, 2 * par:2 * par + 1],
                func=AF.Exp, scale=-0.5),
                reads=[rstd_b], writes=[rstd_b])
            p.emit("vector", lambda e, par=par: e.scalar_tensor_tensor(
                out=rstd[:, 2 * par + 1:2 * par + 2], in0=mv[:, 2 * par:2 * par + 1],
                scalar=fcst[:, 1:2], in1=rstd[:, 2 * par:2 * par + 1],
                op0=ALU.mult, op1=ALU.mult),
                reads=[mv_b, rstd_b, fcst_b], writes=[rstd_b])
            for ap_out, b_out in outs:
                p.emit("vector", lambda e, ap_out=ap_out, xi=xi, par=par: e.tensor_scalar(
                    out=ap_out, in0=xi[:], scalar1=rstd[:, 2 * par:2 * par + 1],
                    scalar2=rstd[:, 2 * par + 1:2 * par + 2],
                    op0=ALU.mult, op1=ALU.add),
                    reads=[xi_b, rstd_b], writes=[b_out])

        def emit_transposes(src, src_b, dstT, dst_bufs, i):
            for j in range(NDC):
                pb_h, pb_b = (pA, pB, pC, pD)[j % 4]
                p.emit("tensor", lambda e, src=src, j=j, pb_h=pb_h: e.transpose(
                    pb_h[:, 0:128], src[:, j * 128:(j + 1) * 128], eye[:]),
                    reads=[src_b, eye_b], writes=[pb_b])
                dst = dstT[:, j * T + i * 128: j * T + (i + 1) * 128]
                if j % 2 == 0:
                    p.emit("vector", lambda e, dst=dst, pb_h=pb_h: e.tensor_copy(
                        out=dst, in_=pb_h[:, 0:128]),
                        reads=[pb_b], writes=[dst_bufs[0]])
                else:
                    p.emit("scalar", lambda e, dst=dst, pb_h=pb_h: e.copy(
                        dst, pb_h[:, 0:128]),
                        reads=[pb_b], writes=[dst_bufs[1]])

        with ExitStack() as ascope:
            xln, xln_b = mk(ascope, "xln", [128, NTT * D], BF16)
            qT, qT_b = mk(ascope, "qT", [128, NPAIR * T], BF16)
            kT, kT_b = mk(ascope, "kT", [128, NPAIR * T], BF16)
            vo0, vo0_b = mk(ascope, "vo0", [128, NDC * D], BF16)

            def prefetch_vo(h_next):
                p.emit("gpsimd", lambda e, h_next=h_next: e.dma_start(
                    out=vo0[:].rearrange("p (c d) -> p c d", c=NDC),
                    in_=vo_h[h_next].rearrange("(c p) d -> p c d", p=128)),
                    writes=[vo0_b], dma=True)

            for s in range(SEQ):
                with ExitStack() as a1:
                    xlnT, xlnT_b = mk(a1, f"xlnT{s}", [128, NDC * T], BF16)
                    xlnT_b2 = Buf(f"xlnT{s}_act")
                    xins = [mk(a1, f"a1xin{k}_{s}", [128, D], F32) for k in range(4)]
                    xns = [mk(a1, f"a1xn{k}_{s}", [128, D], F32) for k in range(4)]
                    qtw0, qtw0_b = mk(a1, f"qtw0{s}", [128, 1024], BF16)
                    qtw1, qtw1_b = mk(a1, f"qtw1{s}", [128, 1024], BF16)
                    qtws = [(qtw0, qtw0_b), (qtw1, qtw1_b)]
                    ktw0, ktw0_b = mk(a1, f"ktw0{s}", [128, 1024], BF16)
                    ktw1, ktw1_b = mk(a1, f"ktw1{s}", [128, 1024], BF16)
                    ktws = [(ktw0, ktw0_b), (ktw1, ktw1_b)]
                    sq, sq_b = mk(a1, f"sq{s}", [128, T], BF16)

                    for i in range(NTT):
                        xi, xi_b = xins[i % 4]
                        xnh, xn_b = xns[i % 4]
                        p.emit("sync", lambda e, s=s, i=i, xi=xi: e.dma_start(
                            out=xi[:], in_=x_h[s, i * 128:(i + 1) * 128, :]),
                            writes=[xi_b], dma=True)
                        emit_ln_tile(xi, xi_b, [
                            (xnh[:], xn_b),
                            (xln[:, i * D:(i + 1) * D], xln_b)], i % 2)
                        emit_transposes(xnh, xn_b, xlnT, (xlnT_b, xlnT_b2), i)

                    def norm_block(pr):
                        for src, src_b, off in ((qT, qT_b, 0), (kT, kT_b, 1)):
                            p.emit("scalar", lambda e, src=src, pr=pr: e.activation(
                                out=sq[:], in_=src[:, pr * T:(pr + 1) * T],
                                func=AF.Square),
                                reads=[src_b], writes=[sq_b])
                            for half in range(2):
                                h = pr * 2 + half
                                for n in range(4):
                                    pn_h, pn_b = (pG, pH)[n % 2]
                                    p.emit("tensor", lambda e, half=half, n=n, pn_h=pn_h: e.matmul(
                                        pn_h[0:1, :], ones[:, 1 + half:2 + half],
                                        sq[:, n * 512:(n + 1) * 512],
                                        start=True, stop=True),
                                        reads=[ones_b, sq_b], writes=[pn_b])
                                    dst_sl = (slice(4 * h + off, 4 * h + off + 1)
                                              if n == 0 else slice(4 * h + 2, 4 * h + 3))
                                    p.emit("vector", lambda e, dst_sl=dst_sl, pn_h=pn_h: e.tensor_reduce(
                                        cg2[0:1, dst_sl], pn_h[0:1, :], AX.X, ALU.max),
                                        reads=[pn_b], writes=[cg2_b])
                                    if n > 0:
                                        p.emit("vector", lambda e, h=h, off=off: e.tensor_tensor(
                                            cg2[0:1, 4 * h + off: 4 * h + off + 1],
                                            cg2[0:1, 4 * h + off: 4 * h + off + 1],
                                            cg2[0:1, 4 * h + 2: 4 * h + 3], ALU.max),
                                            reads=[cg2_b], writes=[cg2_b])

                    for pr in range(NPAIR):
                        qw, qw_b = qtws[pr % 2]
                        kw, kw_b = ktws[pr % 2]
                        p.emit("gpsimd", lambda e, pr=pr, qw=qw: e.dma_start(
                            out=qw[:].rearrange("p (c r) -> p c r", c=NDC),
                            in_=qt_h[pr].rearrange("(c p) r -> p c r", p=128)),
                            writes=[qw_b], dma=True)
                        p.emit("gpsimd", lambda e, pr=pr, kw=kw: e.dma_start(
                            out=kw[:].rearrange("p (c r) -> p c r", c=NDC),
                            in_=kt_h[pr].rearrange("(c p) r -> p c r", p=128)),
                            writes=[kw_b], dma=True)
                        for wsb, wsb_b, dst, dst_b, scl in (
                                (qw, qw_b, qT, qT_b, SCALE),
                                (kw, kw_b, kT, kT_b, 1.0)):
                            for n in range(4):
                                pp_h, pp_b = (pA, pB)[n % 2]
                                for c in range(NDC):
                                    p.emit("tensor", lambda e, wsb=wsb, c=c, n=n, pp_h=pp_h: e.matmul(
                                        pp_h[:], wsb[:, c * 128:(c + 1) * 128],
                                        xlnT[:, c * T + n * 512: c * T + (n + 1) * 512],
                                        start=(c == 0), stop=(c == NDC - 1)),
                                        reads=[wsb_b, xlnT_b, xlnT_b2], writes=[pp_b],
                                        group=("begin" if c == 0 else
                                               ("end" if c == NDC - 1 else "mid")))
                                p.emit("scalar", lambda e, dst=dst, pr=pr, n=n, scl=scl, pp_h=pp_h: e.activation(
                                    out=dst[:, pr * T + n * 512: pr * T + (n + 1) * 512],
                                    in_=pp_h[:], func=AF.Copy, scale=scl),
                                    reads=[pp_b], writes=[dst_b])
                        if pr > 0:
                            norm_block(pr - 1)
                    norm_block(NPAIR - 1)
                    for h in range(NH):
                        p.emit("vector", lambda e, h=h: e.tensor_tensor(
                            cg2[0:1, 4 * h + 3: 4 * h + 4],
                            cg2[0:1, 4 * h + 0: 4 * h + 1],
                            cg2[0:1, 4 * h + 1: 4 * h + 2], ALU.mult),
                            reads=[cg2_b], writes=[cg2_b])
                    p.emit("scalar", lambda e: e.activation(
                        out=crow[0:1, :], in_=cg2[0:1, :], func=AF.Sqrt),
                        reads=[cg2_b], writes=[crow_b])
                    for h in range(NH):
                        pn_h, pn_b = (pG, pH)[h % 2]
                        p.emit("tensor", lambda e, h=h, pn_h=pn_h: e.matmul(
                            pn_h[:, 0:1], orow[0:1, 128:256],
                            crow[0:1, 4 * h + 3: 4 * h + 4],
                            start=True, stop=True),
                            reads=[orow_b, crow_b], writes=[pn_b])
                        p.emit("vector", lambda e, h=h, pn_h=pn_h: e.tensor_copy(
                            out=negc[:, h:h + 1], in_=pn_h[:, 0:1]),
                            reads=[pn_b], writes=[negc_b])
                p.barrier()

                with ExitStack() as a2:
                    atbufs = [mk(a2, f"at{s}_{i}", [128, 16 * 512], BF16)
                              for i in range(3)]
                    oT, oT_b = mk(a2, f"oT{s}", [128, NDC * 512], BF16)
                    oT_bs = [Buf(f"oT{s}_d{d}") for d in range(NDC)]
                    o2, o2_b = mk(a2, f"o2{s}", [128, 4 * D], F32)
                    xres, xres_b = mk(a2, f"xres{s}", [128, 4 * D], F32)

                    def mk_thunks(tq, pr, gp):
                        th = []
                        for u in range(16):
                            def step(u=u, tq=tq, pr=pr, gp=gp):
                                for hh, (ps_h, ps_b) in ((0, (pA[0], pA[1])),
                                                         (1, (pB[0], pB[1]))):
                                    hb = hh * 64
                                    p.emit("tensor", lambda e, pr=pr, hb=hb, u=u, tq=tq, ps_h=ps_h: e.matmul(
                                        ps_h[:],
                                        kT[hb:hb + 64, pr * T + u * 128: pr * T + (u + 1) * 128],
                                        qT[hb:hb + 64, pr * T + tq * 512: pr * T + (tq + 1) * 512],
                                        start=True, stop=True, skip_group_check=True),
                                        reads=[kT_b, qT_b], writes=[ps_b])
                                for hh, (ps_h, ps_b) in ((0, (pA[0], pA[1])),
                                                         (1, (pB[0], pB[1]))):
                                    h = pr * 2 + hh
                                    at, at_b = atbufs[(2 * gp + hh) % 3]
                                    p.emit("scalar", lambda e, u=u, h=h, ps_h=ps_h, at=at: e.activation(
                                        out=at[:, u * 512:(u + 1) * 512], in_=ps_h[:],
                                        func=AF.Exp, bias=negc[:, h:h + 1]),
                                        reads=[ps_b, negc_b], writes=[at_b])
                            th.append(step)
                        return th

                    queue = []

                    def drain(n=1):
                        for _ in range(min(n, len(queue))):
                            queue.pop(0)()

                    def consume(tq, pr, gp):
                        for hh in range(2):
                            h = pr * 2 + hh
                            at, at_b = atbufs[(2 * gp + hh) % 3]
                            for g in range(9):
                                po_h, po_b = (pD, pE)[g % 2]
                                for u in range(16):
                                    if g == 0:
                                        lhs, lhs_b = oblk[:], oblk_b
                                    else:
                                        off = (g - 1) * 128
                                        lhs = xln[:, u * D + off: u * D + off + 128]
                                        lhs_b = xln_b
                                    p.emit("tensor", lambda e, lhs=lhs, u=u, po_h=po_h, at=at: e.matmul(
                                        po_h[:], lhs,
                                        at[:, u * 512:(u + 1) * 512],
                                        start=(u == 0), stop=(u == 15)),
                                        reads=[lhs_b, at_b], writes=[po_b],
                                        group=("begin" if u == 0 else
                                               ("end" if u == 15 else "mid")))
                                if g == 0:
                                    def _recip(e):
                                        with nc.allow_low_precision(
                                                reason="softmax 1/l divisor bf16;"
                                                       " consistent w/ bf16 attnT"):
                                            return e.reciprocal(rrow[0:1, :], pD[0][0:1, :])
                                    p.emit("vector", _recip,
                                           reads=[pD[1]], writes=[rrow_b])
                                elif g == 1:
                                    p.emit("tensor", lambda e: e.matmul(
                                        pC[0][:], orow[0:1, 0:128], rrow[0:1, :],
                                        start=True, stop=True),
                                        reads=[orow_b, rrow_b], writes=[pC[1]])
                                    p.emit("vector", lambda e: e.tensor_copy(
                                        out=f_sb[:], in_=pC[0][:]),
                                        reads=[pC[1]], writes=[f_b])
                                if g >= 1:
                                    p.emit("vector", lambda e, g=g, po_h=po_h: e.tensor_mul(
                                        oT[:, (g - 1) * 512: g * 512], po_h[:], f_sb[:]),
                                        reads=[po_b, f_b], writes=[oT_bs[g - 1]])
                                if hh == 1:
                                    drain(1)
                            for tt in range(4):
                                for eh in range(2):
                                    py_h, py_b = (pF, pG)[(tt * 2 + eh) % 2]
                                    for d in range(NDC):
                                        p.emit("tensor", lambda e, tt=tt, eh=eh, d=d, py_h=py_h: e.matmul(
                                            py_h[:],
                                            oT[:, d * 512 + tt * 128: d * 512 + (tt + 1) * 128],
                                            vo0[:, d * D + eh * 512: d * D + eh * 512 + 512],
                                            start=(d == 0), stop=(d == NDC - 1)),
                                            reads=[oT_bs[d], vo0_b], writes=[py_b],
                                            group=("begin" if d == 0 else
                                                   ("end" if d == NDC - 1 else "mid")))
                                    dsl = slice(tt * D + eh * 512, tt * D + eh * 512 + 512)
                                    if h == 0:
                                        p.emit("vector", lambda e, dsl=dsl, py_h=py_h: e.tensor_copy(
                                            out=o2[:, dsl], in_=py_h[:]),
                                            reads=[py_b], writes=[o2_b])
                                    else:
                                        p.emit("vector", lambda e, dsl=dsl, py_h=py_h: e.tensor_tensor(
                                            o2[:, dsl], py_h[:], o2[:, dsl], ALU.add),
                                            reads=[py_b, o2_b], writes=[o2_b])
                                    drain(1)
                            prefetch_vo((h + 1) % NH)

                    prefetch_vo(0)
                    units = [(tq, pr) for tq in range(NTQ) for pr in range(NPAIR)]
                    for idx, (tq, pr) in enumerate(units):
                        gp = idx
                        if pr == 0:
                            p.emit("sync", lambda e, s=s, tq=tq: e.dma_start(
                                out=xres[:].rearrange("p (c d) -> p c d", c=4),
                                in_=x_h[s, tq * TQ:(tq + 1) * TQ, :].rearrange(
                                    "(c p) d -> p c d", p=128)),
                                writes=[xres_b], dma=True)
                        if idx == 0:
                            for t_ in mk_thunks(tq, pr, gp):
                                t_()
                        drain(len(queue))
                        if idx + 1 < len(units):
                            queue.extend(mk_thunks(*units[idx + 1], gp + 1))
                        consume(tq, pr, gp)
                        if pr == NPAIR - 1:
                            for tt in range(4):
                                p.emit("vector", lambda e, tt=tt: e.tensor_tensor(
                                    o2[:, tt * D:(tt + 1) * D],
                                    o2[:, tt * D:(tt + 1) * D],
                                    xres[:, tt * D:(tt + 1) * D], ALU.add),
                                    reads=[o2_b, xres_b], writes=[o2_b])
                            p.emit("sync", lambda e, s=s, tq=tq: e.dma_start(
                                out=x1_h[s, tq * TQ:(tq + 1) * TQ, :].rearrange(
                                    "(c p) d -> p c d", p=128),
                                in_=o2[:].rearrange("p (c d) -> p c d", c=4)),
                                reads=[o2_b], writes=[x1_b], dma=True)
                    drain(len(queue))
                p.barrier()

        p.barrier()
        with ExitStack() as fscope:
            xln2Ts = []
            for cc in range(SEQ):
                h_, b_ = mk(fscope, f"xln2T{cc}", [128, NDC * T], BF16)
                xln2Ts.append((h_, b_, Buf(f"xln2T{cc}_act")))
            ysb, ysb_b = mk(fscope, "ysb", [128, NTT * D], F32)
            w1s0, w1s0_b = mk(fscope, "w1s0", [128, NDC * 512], BF16)
            w1s1, w1s1_b = mk(fscope, "w1s1", [128, NDC * 512], BF16)
            w3s0, w3s0_b = mk(fscope, "w3s0", [128, NDC * 512], BF16)
            w3s1, w3s1_b = mk(fscope, "w3s1", [128, NDC * 512], BF16)
            w2s0, w2s0_b = mk(fscope, "w2s0", [128, 4 * D], BF16)
            w2s1, w2s1_b = mk(fscope, "w2s1", [128, 4 * D], BF16)
            wbufs = [((w1s0, w1s0_b), (w3s0, w3s0_b), (w2s0, w2s0_b)),
                     ((w1s1, w1s1_b), (w3s1, w3s1_b), (w2s1, w2s1_b))]
            pt_sb, pt_b = mk(fscope, "pt_sb", [128, 4 * T], BF16)
            pt_bs = [Buf(f"pt_f{ft}") for ft in range(4)]
            sil, sil_b = mk(fscope, "sil", [128, 512], BF16)
            xin2s = [mk(fscope, f"fxin{k}", [128, D], F32) for k in range(1)]
            xn2s = [mk(fscope, f"fxn{k}", [128, D], F32) for k in range(1)]
            lgt, lgt_b = mk(fscope, "lgt", [128, 64], F32)

            def load_expert(e_i, bufs3):
                (a1s, a1b), (a3s, a3b), (a2s, a2b) = bufs3
                for wh, ws, ws_b, ncnk in ((w1_h, a1s, a1b, NDC),
                                           (w3_h, a3s, a3b, NDC),
                                           (w2_h, a2s, a2b, 4)):
                    p.emit("gpsimd", lambda e, wh=wh, ws=ws, e_i=e_i, ncnk=ncnk: e.dma_start(
                        out=ws[:].rearrange("p (c f) -> p c f", c=ncnk),
                        in_=wh[e_i].rearrange("(c p) f -> p c f", p=128)),
                        writes=[ws_b], dma=True)

            def ln_router_thunks(ch):
                xln2T, xln2T_b, xln2T_b2 = xln2Ts[ch]
                th = []
                for i in range(NTT):
                    def t_ln(i=i, ch=ch, xln2T=xln2T, xln2T_b=xln2T_b,
                             xln2T_b2=xln2T_b2):
                        xin2, xin2_b = xin2s[0]
                        xn2, xn2_b = xn2s[0]
                        p.emit("sync", lambda e, ch=ch, i=i, xin2=xin2: e.dma_start(
                            out=xin2[:], in_=x1_h[ch, i * 128:(i + 1) * 128, :]),
                            reads=[x1_b], writes=[xin2_b], dma=True)
                        emit_ln_tile(xin2, xin2_b, [(xn2[:], xn2_b)], i % 2)
                        emit_transposes(xn2, xn2_b, xln2T, (xln2T_b, xln2T_b2), i)
                    th.append(t_ln)
                for i in range(NTT):
                    def t_rt(i=i, ch=ch, xln2T=xln2T, xln2T_b=xln2T_b,
                             xln2T_b2=xln2T_b2):
                        pr_h = (pC, pH)[i % 2]
                        for c in range(NDC):
                            p.emit("tensor", lambda e, i=i, c=c, pr_h=pr_h, xln2T=xln2T: e.matmul(
                                pr_h[0][:, 0:8],
                                xln2T[:, c * T + i * 128: c * T + (i + 1) * 128],
                                rwsb[:, c * 8:(c + 1) * 8],
                                start=(c == 0), stop=(c == NDC - 1)),
                                reads=[xln2T_b, xln2T_b2, rwsb_b], writes=[pr_h[1]],
                                group=("begin" if c == 0 else
                                       ("end" if c == NDC - 1 else "mid")))
                        p.emit("vector", lambda e, pr_h=pr_h: e.tensor_tensor(
                            lgt[:, 0:8], pr_h[0][:, 0:8], rbb[:], ALU.add),
                            reads=[pr_h[1], rbb_b], writes=[lgt_b])
                        p.emit("vector", lambda e: e.tensor_reduce(
                            lgt[:, 8:9], lgt[:, 0:8], AX.X, ALU.max),
                            reads=[lgt_b], writes=[lgt_b])
                        p.emit("vector", lambda e: e.tensor_scalar(
                            out=lgt[:, 16:24], in0=lgt[:, 0:8], scalar1=lgt[:, 8:9],
                            scalar2=None, op0=ALU.is_ge),
                            reads=[lgt_b], writes=[lgt_b])
                        p.emit("vector", lambda e: e.scalar_tensor_tensor(
                            out=lgt[:, 24:32], in0=lgt[:, 16:24], scalar=fcst[:, 2:3],
                            in1=lgt[:, 0:8], op0=ALU.mult, op1=ALU.add),
                            reads=[lgt_b, fcst_b], writes=[lgt_b])
                        p.emit("vector", lambda e: e.tensor_reduce(
                            lgt[:, 9:10], lgt[:, 24:32], AX.X, ALU.max),
                            reads=[lgt_b], writes=[lgt_b])
                        p.emit("vector", lambda e: e.tensor_scalar(
                            out=lgt[:, 10:11], in0=lgt[:, 8:9], scalar1=fcst[:, 1:2],
                            scalar2=None, op0=ALU.mult),
                            reads=[lgt_b, fcst_b], writes=[lgt_b])
                        p.emit("scalar", lambda e: e.activation(
                            out=lgt[:, 32:40], in_=lgt[:, 0:8], func=AF.Exp,
                            bias=lgt[:, 10:11]),
                            reads=[lgt_b], writes=[lgt_b])
                        p.emit("vector", lambda e: e.tensor_scalar(
                            out=lgt[:, 40:48], in0=lgt[:, 0:8], scalar1=lgt[:, 9:10],
                            scalar2=None, op0=ALU.is_ge),
                            reads=[lgt_b], writes=[lgt_b])
                        p.emit("vector", lambda e: e.tensor_tensor(
                            lgt[:, 48:56], lgt[:, 32:40], lgt[:, 40:48], ALU.mult),
                            reads=[lgt_b], writes=[lgt_b])
                        p.emit("vector", lambda e: e.tensor_reduce(
                            lgt[:, 11:12], lgt[:, 48:56], AX.X, ALU.add),
                            reads=[lgt_b], writes=[lgt_b])
                        p.emit("vector", lambda e: e.reciprocal(
                            lgt[:, 12:13], lgt[:, 11:12]),
                            reads=[lgt_b], writes=[lgt_b])
                        p.emit("vector", lambda e, i=i, ch=ch: e.tensor_scalar(
                            out=gates[:, ch * NTT * 8 + i * 8: ch * NTT * 8 + (i + 1) * 8],
                            in0=lgt[:, 48:56],
                            scalar1=lgt[:, 12:13], scalar2=None, op0=ALU.mult),
                            reads=[lgt_b], writes=[gates_b])
                    th.append(t_rt)
                return th

            fqueue = []

            def fdrain(n=1):
                for _ in range(min(n, len(fqueue))):
                    fqueue.pop(0)()

            for ch in range(SEQ):
                xln2T, xln2T_b, xln2T_b2 = xln2Ts[ch]
                if ch == 0:
                    for t_ in ln_router_thunks(0):
                        t_()
                else:
                    fdrain(len(fqueue))
                for q in range(4):
                    p.emit("sync", lambda e, ch=ch, q=q: e.dma_start(
                        out=ysb[:, q * 4 * D:(q + 1) * 4 * D].rearrange(
                            "p (c d) -> p c d", c=4),
                        in_=x1_h[ch, q * TQ:(q + 1) * TQ, :].rearrange(
                            "(c p) d -> p c d", p=128)),
                        reads=[x1_b], writes=[ysb_b], dma=True)
                if ch + 1 < SEQ:
                    fqueue.extend(ln_router_thunks(ch + 1))

                for e_i in range(NE):
                    bufs3 = wbufs[e_i % 2]
                    (w1s, w1s_b), (w3s, w3s_b), (w2s, w2s_b) = bufs3
                    if e_i == 0:
                        load_expert(0, bufs3)
                    if e_i < NE - 1:
                        load_expert(e_i + 1, wbufs[(e_i + 1) % 2])
                    for ft in range(4):
                        for ts in range(4):
                            p1_h, p1_b = (pA, pB)[ts % 2]
                            p3_h, p3_b = (pD, pE)[ts % 2]
                            for ws, ws_b, pp_h, pp_b in ((w1s, w1s_b, p1_h, p1_b),
                                                         (w3s, w3s_b, p3_h, p3_b)):
                                for c in range(NDC):
                                    p.emit("tensor", lambda e, ws=ws, ft=ft, ts=ts, c=c, pp_h=pp_h, xln2T=xln2T: e.matmul(
                                        pp_h[:],
                                        ws[:, c * 512 + ft * 128: c * 512 + (ft + 1) * 128],
                                        xln2T[:, c * T + ts * 512: c * T + (ts + 1) * 512],
                                        start=(c == 0), stop=(c == NDC - 1)),
                                        reads=[ws_b, xln2T_b, xln2T_b2], writes=[pp_b],
                                        group=("begin" if c == 0 else
                                               ("end" if c == NDC - 1 else "mid")))
                            p.emit("scalar", lambda e, p1_h=p1_h: e.activation(
                                out=sil[:], in_=p1_h[:], func=AF.Silu),
                                reads=[p1_b], writes=[sil_b])
                            p.emit("vector", lambda e, ft=ft, ts=ts, p3_h=p3_h: e.tensor_tensor(
                                pt_sb[:, ft * T + ts * 512: ft * T + (ts + 1) * 512],
                                sil[:], p3_h[:], ALU.mult),
                                reads=[sil_b, p3_b], writes=[pt_bs[ft]])
                        fdrain(1)
                    for tt in range(NTT):
                        for eh in range(2):
                            py_h, py_b = (pF, pG)[(tt * 2 + eh) % 2]
                            for fc in range(4):
                                p.emit("tensor", lambda e, tt=tt, eh=eh, fc=fc, w2s=w2s, py_h=py_h: e.matmul(
                                    py_h[:],
                                    pt_sb[:, fc * T + tt * 128: fc * T + (tt + 1) * 128],
                                    w2s[:, fc * D + eh * 512: fc * D + eh * 512 + 512],
                                    start=(fc == 0), stop=(fc == 3)),
                                    reads=[pt_bs[fc], w2s_b], writes=[py_b],
                                    group=("begin" if fc == 0 else
                                           ("end" if fc == 3 else "mid")))
                            dsl = slice(tt * D + eh * 512, tt * D + eh * 512 + 512)
                            if e_i < 8:
                                p.emit("vector", lambda e, dsl=dsl, tt=tt, e_i=e_i, ch=ch, py_h=py_h: e.scalar_tensor_tensor(
                                    out=ysb[:, dsl], in0=py_h[:],
                                    scalar=gates[:, ch * NTT * 8 + tt * 8 + e_i: ch * NTT * 8 + tt * 8 + e_i + 1],
                                    in1=ysb[:, dsl], op0=ALU.mult, op1=ALU.add),
                                    reads=[py_b, gates_b, ysb_b], writes=[ysb_b])
                            else:
                                p.emit("vector", lambda e, dsl=dsl, py_h=py_h: e.tensor_tensor(
                                    ysb[:, dsl], py_h[:], ysb[:, dsl], ALU.add),
                                    reads=[py_b, ysb_b], writes=[ysb_b])
                        fdrain(1)
                for q in range(4):
                    p.emit("sync", lambda e, ch=ch, q=q: e.dma_start(
                        out=out_h[ch, q * TQ:(q + 1) * TQ, :].rearrange(
                            "(c p) d -> p c d", p=128),
                        in_=ysb[:, q * 4 * D:(q + 1) * 4 * D].rearrange(
                            "p (c d) -> p c d", c=4)),
                        reads=[ysb_b], writes=[out_b], dma=True)

        p.barrier()

        sems = {s: top.enter_context(nc.semaphore(s)) for s in sorted(p.counts)}
        with nc.Block() as block:
            def run_stream(eng_obj, eng_name):
                for waits, fn, incs in p.ops[eng_name]:
                    for s_name, val in waits:
                        eng_obj.wait_ge(sems[s_name], val)
                    if fn is not None:
                        ins = fn(eng_obj)
                        for s_name, amt in incs:
                            ins.then_inc(sems[s_name], amt)

            @block.tensor
            def _(t):
                run_stream(t, "tensor")

            @block.vector
            def _(v):
                run_stream(v, "vector")

            @block.scalar
            def _(s_):
                run_stream(s_, "scalar")

            @block.gpsimd
            def _(g):
                run_stream(g, "gpsimd")

            @block.sync
            def _(sy):
                run_stream(sy, "sync")

    return nc


_CACHED = {}


def _get_program():
    if "nc" not in _CACHED:
        _CACHED["nc"] = build_program()
    return _CACHED["nc"]


def kernel(L0_attn_x, L0_attn_Q, L0_attn_K, L0_attn_VO,
           L0_ffn_router_w, L0_ffn_router_b,
           L0_ffn_routed_w1, L0_ffn_routed_w3, L0_ffn_routed_w2,
           L0_ffn_shared_w1, L0_ffn_shared_w3, L0_ffn_shared_w2):
    x = np.ascontiguousarray(np.asarray(L0_attn_x, dtype=np.float32))
    Q = np.asarray(L0_attn_Q, dtype=np.float32)
    K = np.asarray(L0_attn_K, dtype=np.float32)
    VO = np.ascontiguousarray(np.asarray(L0_attn_VO, dtype=np.float32))

    def pack_qk(w):
        wt = w.transpose(0, 2, 1).reshape(NPAIR, 2, D, RK)
        return np.ascontiguousarray(wt.transpose(0, 2, 1, 3).reshape(NPAIR, D, 128))

    qt = pack_qk(Q)
    kt = pack_qk(K)
    rw = np.ascontiguousarray(np.asarray(L0_ffn_router_w, dtype=np.float32))
    rb = np.ascontiguousarray(np.broadcast_to(
        np.asarray(L0_ffn_router_b, dtype=np.float32)[None, :], (128, 8)))
    w1 = np.ascontiguousarray(np.concatenate(
        [np.asarray(L0_ffn_routed_w1, np.float32),
         np.asarray(L0_ffn_shared_w1, np.float32)], axis=0))
    w3 = np.ascontiguousarray(np.concatenate(
        [np.asarray(L0_ffn_routed_w3, np.float32),
         np.asarray(L0_ffn_shared_w3, np.float32)], axis=0))
    w2 = np.ascontiguousarray(np.concatenate(
        [np.asarray(L0_ffn_routed_w2, np.float32),
         np.asarray(L0_ffn_shared_w2, np.float32)], axis=0))
    eye = np.eye(128, dtype=np.float32)
    ones = np.zeros((128, 4), dtype=bfdt)
    ones[:, 0] = 1
    ones[0:64, 1] = 1
    ones[64:128, 2] = 1
    orow = np.zeros((1, 256), dtype=bfdt)
    orow[0, 0:128] = 1
    orow[0, 128:256] = -1
    oblk = np.zeros((128, 128), dtype=bfdt)
    oblk[:, 0] = 1
    fconst = np.zeros((128, 3), dtype=np.float32)
    fconst[:, 0] = EPS
    fconst[:, 1] = -1.0
    fconst[:, 2] = -1e30

    nc = _get_program()
    shared = {"qt": qt, "kt": kt, "vo": VO, "rw": rw, "rb": rb,
              "w1": w1, "w3": w3, "w2": w2, "eye": eye, "ones": ones,
              "orow": orow, "fconst": fconst, "oblk": oblk}
    in_maps = [dict(shared, x=np.ascontiguousarray(x[c * SEQ:(c + 1) * SEQ]))
               for c in range(NCORES)]
    res = run_bass_kernel_spmd(nc, in_maps, core_ids=list(range(NCORES)))
    out = np.concatenate([np.asarray(r["out"]) for r in res.results], axis=0)
    return out.astype(np.float32)
